# version 05 (final)
# baseline (speedup 1.0000x reference)
import numpy as np
import ml_dtypes
from contextlib import ExitStack

import concourse.bass as bass
import concourse.mybir as mybir
from concourse.bass_utils import run_bass_kernel_spmd

F32 = mybir.dt.float32
BF16 = mybir.dt.bfloat16
AF = mybir.ActivationFunctionType
ALU = mybir.AluOpType
AX = mybir.AxisListType

B, S, D = 4, 4096, 2048
H, G, DH = 16, 4, 128
FF0 = 8192
FS, FR, NE = 2816, 1408, 8
EPS = 1e-5
NCORE = 8
TPC = 2048
KC = D // 128
ISCALE = float(1.0 / np.sqrt(128.0))
NEGM = -30000.0

bfdt = ml_dtypes.bfloat16


def tile_w(wT):
    K, M = wT.shape
    return np.ascontiguousarray(wT.reshape(K, M // 128, 128).transpose(1, 0, 2))


def cc(a):
    return np.ascontiguousarray(a)


def fm_load(eng, dst_sb, dst_off, src_dram, tlen, sem):
    eng.dma_start(dst_sb[:, dst_off:dst_off + tlen], src_dram).then_inc(sem, 16)


class NCView:

    def __init__(self, nc, pfx):
        self._nc = nc
        self._pfx = pfx

    def sbuf_tensor(self, name, *a, **k):
        return self._nc.sbuf_tensor(self._pfx + name, *a, **k)

    def psum_tensor(self, name, *a, **k):
        return self._nc.psum_tensor(self._pfx + name, *a, **k)

    def semaphore(self, name=None, num=None):
        import contextlib
        h = self._nc.alloc_semaphore(name=self._pfx + (name or "s"))
        return contextlib.nullcontext(h)

    def __getattr__(self, n):
        return getattr(self._nc, n)


def clear_sems(nc):
    kr = bass.get_kernel_semaphore_range()
    n_skip = 2
    if getattr(nc, "_bir_kernel_barrier_sem", None) is not None:
        n_skip += 1
    n_skip += len(getattr(nc, "_monotonic_sems", []))
    rng = range(kr.start + n_skip, kr.stop)
    with nc.Block() as b:
        @b.gpsimd
        def _(g):
            g.sem_clear(rng)


def warmup(nc, ex, ps_ap=None):
    dmy = ex.enter_context(nc.sbuf_tensor("dmy_sb", [128, 128], BF16))
    wst = ex.enter_context(nc.sbuf_tensor("wst_sb", [128, 8], F32))
    exl = ExitStack()
    if ps_ap is None:
        ps_w = exl.enter_context(nc.psum_tensor("ps_wrm", [128, 512], F32))
        ps_ap = ps_w[:, 0:128]
    sw = ex.enter_context(nc.semaphore("s_wrm"))
    with nc.Block() as b:
        @b.gpsimd
        def _(g):
            g.memset(dmy[:, :], 0.0)
            g.memset(wst[:, :], 0.0)

        @b.tensor
        def _(t):
            t.matmul(ps_ap, dmy[:, :], dmy[:, :], start=True, stop=True
                     ).then_inc(sw, 1)

        @b.vector
        def _(ve):
            ve.wait_ge(sw, 1)
            ve.reduce_max(wst[:, 0:1], ps_ap, axis=AX.X)
            ve.tensor_copy(wst[:, 2:3], wst[:, 0:1])

        @b.scalar
        def _(sc):
            sc.activation(wst[:, 3:4], wst[:, 4:5], AF.Exp, bias=0.0, scale=0.0)
    exl.close()
    return dmy, ps_ap


def setup_consts(nc, ex, values):
    tile = ex.enter_context(nc.sbuf_tensor("cconsts", [128, max(1, len(values))], F32))
    with nc.Block() as b:
        @b.gpsimd
        def _(g):
            for i, v in enumerate(values):
                g.memset(tile[:, i:i + 1], float(v))
    return {v: tile[:, i:i + 1] for i, v in enumerate(values)}


def build_ln(T, ncv=None, io=None):
    if ncv is None:
        nc = bass.Bass("TRN2", target_bir_lowering=False, detect_race_conditions=False)
        x = nc.dram_tensor("x", [T, D], F32, kind="ExternalInput")
        xn = nc.dram_tensor("xn", [T, D], F32, kind="ExternalOutput")
    else:
        nc = ncv
        x, xn = io["x"], io["xn"]
    NT = T // 128
    with ExitStack() as ex:
        xt = ex.enter_context(nc.sbuf_tensor("xt", [128, 2 * D], F32))
        xc = ex.enter_context(nc.sbuf_tensor("xc", [128, 2 * D], F32))
        sq = ex.enter_context(nc.sbuf_tensor("sq", [128, D], F32))
        on = ex.enter_context(nc.sbuf_tensor("on", [128, 2 * D], F32))
        st = ex.enter_context(nc.sbuf_tensor("st", [128, 12], F32))
        s_ld = ex.enter_context(nc.semaphore("s_ld"))
        s_xc = ex.enter_context(nc.semaphore("s_xc"))
        s_ac = ex.enter_context(nc.semaphore("s_ac"))
        s_on = ex.enter_context(nc.semaphore("s_on"))
        s_st = ex.enter_context(nc.semaphore("s_st"))
        if io is None:
            warmup(nc, ex, None)
            cst = setup_consts(nc, ex, [1.0 / D, EPS])
        else:
            cst = io["cst"]
        blk = ex.enter_context(nc.Block())

        @blk.sync
        def _(sy):
            for i in range(NT):
                if i >= 2:
                    sy.wait_ge(s_on, i - 1)
                b = i % 2
                sy.dma_start(xt[:, b * D:(b + 1) * D],
                             x[i * 128:(i + 1) * 128, :]).then_inc(s_ld, 16)

        @blk.vector
        def _(v):
            for i in range(NT):
                b = i % 2
                o = b * 6
                v.wait_ge(s_ld, 16 * (i + 1))
                v.reduce_sum(st[:, o:o + 1], xt[:, b * D:(b + 1) * D], axis=AX.X)
                v.tensor_scalar(st[:, o + 1:o + 2], st[:, o:o + 1],
                                cst[1.0 / D], None, ALU.mult)
                if i >= 2:
                    v.wait_ge(s_ac, i - 1)
                v.drain()
                v.tensor_scalar(xc[:, b * D:(b + 1) * D], xt[:, b * D:(b + 1) * D],
                                st[:, o + 1:o + 2], None, ALU.subtract)
                v.drain().then_inc(s_xc, 1)
                v.wait_ge(s_ac, i + 1)
                v.reciprocal(st[:, o + 3:o + 4], st[:, o + 2:o + 3])
                if i >= 2:
                    v.wait_ge(s_st, 16 * (i - 1))
                v.drain()
                v.tensor_scalar(on[:, b * D:(b + 1) * D], xt[:, b * D:(b + 1) * D],
                                st[:, o + 1:o + 2], st[:, o + 3:o + 4],
                                ALU.subtract, ALU.mult)
                v.drain().then_inc(s_on, 1)

        @blk.scalar
        def _(sc):
            for i in range(NT):
                b = i % 2
                o = b * 6
                sc.wait_ge(s_xc, i + 1)
                sc.activation(sq[:, :], xc[:, b * D:(b + 1) * D], AF.Square,
                              accum_out=st[:, o + 2:o + 3])
                sc.drain()
                sc.activation(st[:, o + 2:o + 3], st[:, o + 2:o + 3], AF.Sqrt,
                              scale=1.0 / D, bias=cst[EPS])
                sc.drain().then_inc(s_ac, 1)

        @blk.gpsimd
        def _(g):
            for i in range(NT):
                b = i % 2
                g.wait_ge(s_on, i + 1)
                g.dma_start(xn[i * 128:(i + 1) * 128, :],
                            on[:, b * D:(b + 1) * D]).then_inc(s_st, 16)
            g.wait_ge(s_st, 16 * NT)
    return nc if ncv is None else None


def build_qkv(ncv=None, io=None, T=TPC):
    if ncv is None:
        nc = bass.Bass("TRN2", target_bir_lowering=False, detect_race_conditions=False)
        xn = nc.dram_tensor("xn", [D, T], F32, kind="ExternalInput")
        wq = nc.dram_tensor("wq", [H, KC, 128, 128], F32, kind="ExternalInput")
        wk = nc.dram_tensor("wk", [G, KC, 128, 128], F32, kind="ExternalInput")
        wv = nc.dram_tensor("wv", [G, KC, 128, 128], F32, kind="ExternalInput")
        bqT = nc.dram_tensor("bqT", [128, H], F32, kind="ExternalInput")
        bkT = nc.dram_tensor("bkT", [128, G], F32, kind="ExternalInput")
        bv = nc.dram_tensor("bv", [1, G * 128], F32, kind="ExternalInput")
        q = nc.dram_tensor("q", [H, 128, T], BF16, kind="ExternalOutput")
        k = nc.dram_tensor("k", [G, 128, T], BF16, kind="ExternalOutput")
        v = nc.dram_tensor("v", [G, T, 128], BF16, kind="ExternalOutput")
    else:
        nc = ncv
        xn, wq, wk, wv, bqT, bkT, bv, q, k, v = (io[n] for n in (
            "xn", "wq", "wk", "wv", "bqT", "bkT", "bv", "q", "k", "v"))

    TC = T // 512
    NTT = T // 128
    wlist = [("q", h) for h in range(H)] + [("k", g) for g in range(G)] + \
            [("v", g) for g in range(G)]
    ntiles = {"q": TC, "k": TC, "v": NTT}
    cum = []
    c = 0
    for kind, idx in wlist:
        c += ntiles[kind]
        cum.append(c)
    n_psum = c

    tile_meta = []
    for kind, idx in wlist:
        for i in range(ntiles[kind]):
            tile_meta.append((kind, idx, i))

    with ExitStack() as ex:
        xn_sb = ex.enter_context(nc.sbuf_tensor("xn_sb", [128, KC * T], BF16))
        w_sb = ex.enter_context(nc.sbuf_tensor("w_sb", [128, 2 * KC * 128], BF16))
        o_sb = ex.enter_context(nc.sbuf_tensor("o_sb", [128, 4 * 512], BF16))
        bq_sb = ex.enter_context(nc.sbuf_tensor("bq_sb", [128, H], F32))
        bk_sb = ex.enter_context(nc.sbuf_tensor("bk_sb", [128, G], F32))
        bvc = ex.enter_context(nc.sbuf_tensor("bvc", [128, G * 128], F32))
        ps_t8 = ex.enter_context(nc.psum_tensor("ps_tail", [128, 512], F32))
        ps_wrm = ps_t8[:, 0:128]
        if io is None:
            dmy, _ = warmup(nc, ex, ps_wrm)
        else:
            dmy = io["dmy"]
        ps = ex.enter_context(nc.psum_tensor("ps", [128, 7 * 512], F32))
        s_xn = ex.enter_context(nc.semaphore("s_xn"))
        s_w = ex.enter_context(nc.semaphore("s_w"))
        s_mm = ex.enter_context(nc.semaphore("s_mm"))
        s_ev = ex.enter_context(nc.semaphore("s_ev"))
        s_vv = ex.enter_context(nc.semaphore("s_vv"))
        s_st = ex.enter_context(nc.semaphore("s_st"))
        s_b = ex.enter_context(nc.semaphore("s_b"))
        blk = ex.enter_context(nc.Block())

        free_sem = []
        nev = nvv = 0
        for kind, idx, i in tile_meta:
            if kind == "v":
                nvv += 1
                free_sem.append(("vv", nvv))
            else:
                nev += 1
                free_sem.append(("ev", nev))

        @blk.gpsimd
        def _(g):
            g.dma_start(bvc[:, :], bass.AP(bv, 0, [[0, 128], [1, G * 128]])
                        ).then_inc(s_xn, 16)
            for kc in range(KC):
                fm_load(g, xn_sb, kc * T, xn[kc * 128:(kc + 1) * 128, :], T, s_xn)
            g.dma_start(dmy[:, :], xn[0:128, 0:128]).then_inc(s_xn, 16)
            for j, (kind, idx) in enumerate(wlist):
                if j >= 2:
                    g.wait_ge(s_mm, cum[j - 2])
                buf = j % 2
                src = {"q": wq, "k": wk, "v": wv}[kind]
                g.dma_start(
                    w_sb[:, buf * KC * 128:(buf + 1) * KC * 128]
                    .rearrange("p (kc m) -> p kc m", m=128),
                    src[idx].rearrange("kc p m -> p kc m"),
                ).then_inc(s_w, 16)
            g.dma_start(dmy[:, :], xn[0:128, 0:128]).then_inc(s_w, 16)

        @blk.sync
        def _(sy):
            sy.dma_start(bq_sb[:, :], bqT[:, :]).then_inc(s_b, 16)
            sy.dma_start(bk_sb[:, :], bkT[:, :]).then_inc(s_b, 16)

        @blk.tensor
        def _(t):
            t.wait_ge(s_xn, 16 * (KC + 1))
            t.wait_ge(s_b, 32)
            cnt = 0
            for j, (kind, idx) in enumerate(wlist):
                t.wait_ge(s_w, 16 * (j + 1))
                buf = j % 2
                wb = buf * KC * 128
                if kind in ("q", "k"):
                    for tc in range(TC):
                        bank = cnt % 7
                        if cnt >= 7:
                            w_sem, w_val = free_sem[cnt - 7]
                            t.wait_ge(s_ev if w_sem == "ev" else s_vv, w_val)
                        for kc in range(KC):
                            inst = t.matmul(
                                ps[:, bank * 512:(bank + 1) * 512],
                                w_sb[:, wb + kc * 128:wb + (kc + 1) * 128],
                                xn_sb[:, kc * T + tc * 512:kc * T + (tc + 1) * 512],
                                start=(kc == 0), stop=(kc == KC - 1))
                        inst.then_inc(s_mm, 1)
                        cnt += 1
                else:
                    for tt in range(NTT):
                        bank = cnt % 7
                        if cnt >= 7:
                            w_sem, w_val = free_sem[cnt - 7]
                            t.wait_ge(s_ev if w_sem == "ev" else s_vv, w_val)
                        for kc in range(KC):
                            inst = t.matmul(
                                ps[:, bank * 512:bank * 512 + 128],
                                xn_sb[:, kc * T + tt * 128:kc * T + (tt + 1) * 128],
                                w_sb[:, wb + kc * 128:wb + (kc + 1) * 128],
                                start=(kc == 0), stop=(kc == KC - 1))
                        inst.then_inc(s_mm, 1)
                        cnt += 1
            t.matmul(ps_wrm, dmy[:, :], dmy[:, :], start=True,
                     stop=True).then_inc(s_mm, 1)

        @blk.vector
        def _(ve):
            cnt = 0
            myv = 0
            for kind, idx, i in tile_meta:
                if kind == "v":
                    bank = cnt % 7
                    ve.wait_ge(s_mm, cnt + 2)
                    if cnt >= 4:
                        ve.wait_ge(s_st, 16 * (cnt - 3))
                    ob = cnt % 4
                    ve.tensor_add(o_sb[:, ob * 512:ob * 512 + 128],
                                  ps[:, bank * 512:bank * 512 + 128],
                                  bvc[:, idx * 128:(idx + 1) * 128])
                    ve.drain().then_inc(s_vv, 1)
                    myv += 1
                cnt += 1

        @blk.scalar
        def _(sc):
            cnt = 0
            for kind, idx, i in tile_meta:
                bank = cnt % 7
                ob = cnt % 4
                if kind in ("q", "k"):
                    sc.wait_ge(s_mm, cnt + 2)
                    if cnt >= 4:
                        sc.wait_ge(s_st, 16 * (cnt - 3))
                    bias = bq_sb[:, idx:idx + 1] if kind == "q" else bk_sb[:, idx:idx + 1]
                    sc.activation(o_sb[:, ob * 512:ob * 512 + 512],
                                  ps[:, bank * 512:bank * 512 + 512],
                                  AF.Identity, bias=bias)
                    sc.drain().then_inc(s_ev, 1)
                cnt += 1

        @blk.sync
        def _(sy):
            cnt = 0
            nev2 = nvv2 = 0
            for kind, idx, i in tile_meta:
                ob = cnt % 4
                if kind in ("q", "k"):
                    w = 512
                    nev2 += 1
                    sy.wait_ge(s_ev, nev2)
                    dst = (q if kind == "q" else k)[idx, :, i * 512:(i + 1) * 512]
                else:
                    w = 128
                    nvv2 += 1
                    sy.wait_ge(s_vv, nvv2)
                    dst = v[idx, i * 128:(i + 1) * 128, :]
                sy.dma_start(dst, o_sb[:, ob * 512:ob * 512 + w]).then_inc(s_st, 16)
                cnt += 1
            sy.wait_ge(s_st, 16 * n_psum)
            sy.wait_ge(s_xn, 16 * (KC + 2))
            sy.wait_ge(s_w, 16 * (len(wlist) + 1))
    return nc if ncv is None else None


def build_attn(nqt=None, nheads=8, ngrp=2, ncv=None, io=None):
    if ncv is None:
        nc = bass.Bass("TRN2", target_bir_lowering=False, detect_race_conditions=False)
        q = nc.dram_tensor("q", [nheads, 128, S], BF16, kind="ExternalInput")
        kt = nc.dram_tensor("kt", [ngrp, 128, S], BF16, kind="ExternalInput")
        v = nc.dram_tensor("v", [ngrp, S, 128], BF16, kind="ExternalInput")
        s_out = nc.dram_tensor("s_out", [nheads * 128, S], F32,
                               kind="ExternalOutput")
    else:
        nc = ncv
        q, kt, v, s_out = (io[n] for n in ("q", "kt", "v", "s_out"))

    NQT = nqt or (S // 128)
    tiles = [(h, qt) for h in range(nheads) for qt in range(NQT)]
    rows = [(qt + 1) * 128 for h, qt in tiles]
    chs = [(r + 511) // 512 for r in rows]
    grs = [(qt + 1 + 7) // 8 for h, qt in tiles]
    cum_ch = np.cumsum([0] + chs)
    cum_gr = np.cumsum([0] + grs)

    with ExitStack() as ex:
        q_sb = ex.enter_context(nc.sbuf_tensor("q_sb", [128, 2 * S], BF16))
        kt_sb = ex.enter_context(nc.sbuf_tensor("kt_sb", [128, ngrp * S], BF16))
        v_sb = ex.enter_context(nc.sbuf_tensor("v_sb", [128, ngrp * S], BF16))
        p_sb = ex.enter_context(nc.sbuf_tensor("p_sb", [128, S], BF16))
        pt_sb = ex.enter_context(nc.sbuf_tensor("pt_sb", [128, S], BF16))
        so_sb = ex.enter_context(nc.sbuf_tensor("so_sb", [128, S], F32))
        mask = ex.enter_context(nc.sbuf_tensor("mask", [128, 128], F32))
        ident = ex.enter_context(nc.sbuf_tensor("ident", [128, 128], BF16))
        mx = ex.enter_context(nc.sbuf_tensor("mx", [128, 24], F32))
        ps_s = ex.enter_context(nc.psum_tensor("ps_s", [128, 4 * 512], F32))
        ps_t = ex.enter_context(nc.psum_tensor("ps_t", [128, 2048], BF16))
        ps_o = ex.enter_context(nc.psum_tensor("ps_o", [128, 512], F32))
        s_in = ex.enter_context(nc.semaphore("s_in"))
        s_m1 = ex.enter_context(nc.semaphore("s_m1"))
        s_p1 = ex.enter_context(nc.semaphore("s_p1"))
        s_ng = ex.enter_context(nc.semaphore("s_ng"))
        s_m2 = ex.enter_context(nc.semaphore("s_m2"))
        s_dm = ex.enter_context(nc.semaphore("s_dm"))
        s_ex = ex.enter_context(nc.semaphore("s_ex"))
        s_exd = ex.enter_context(nc.semaphore("s_exd"))
        s_rr = ex.enter_context(nc.semaphore("s_rr"))
        s_pm = ex.enter_context(nc.semaphore("s_pm"))
        s_tr = ex.enter_context(nc.semaphore("s_tr"))
        s_pc = ex.enter_context(nc.semaphore("s_pc"))
        s_pv = ex.enter_context(nc.semaphore("s_pv"))
        s_se = ex.enter_context(nc.semaphore("s_se"))
        s_st = ex.enter_context(nc.semaphore("s_st"))
        s_qh = ex.enter_context(nc.semaphore("s_qh"))
        s_ini = ex.enter_context(nc.semaphore("s_ini"))
        if io is None:
            dmy, ps_wrm = warmup(nc, ex, ps_s[:, 0:128])
            cst = setup_consts(nc, ex, [-ISCALE])
        else:
            dmy, ps_wrm, cst = io["dmy"], io["ps_wrm"], io["cst"]
        blk = ex.enter_context(nc.Block())

        @blk.gpsimd
        def _(g):
            g.memset(mask[:, :], 0.0)
            g.affine_select(out=mask[:, :], in_=mask[:, :], compare_op=ALU.is_ge,
                            fill=NEGM, base=0, pattern=[[-1, 128]],
                            channel_multiplier=1)
            g.memset(ident[:, :], 0.0)
            g.affine_select(out=ident[:, :], in_=ident[:, :],
                            compare_op=ALU.not_equal, fill=1.0, base=0,
                            pattern=[[-1, 128]], channel_multiplier=1)
            g.engine_nop().then_inc(s_ini, 1)
            for gg in range(ngrp):
                g.dma_start(kt_sb[:, gg * S:(gg + 1) * S], kt[gg]).then_inc(s_in, 16)
                g.dma_start(
                    v_sb[:, gg * S:(gg + 1) * S].rearrange("p (kt d) -> p kt d", d=128),
                    v[gg].rearrange("(kt p) d -> p kt d", p=128),
                ).then_inc(s_in, 16)
            for hh in range(nheads):
                if hh >= 2:
                    g.wait_ge(s_m2, int(cum_ch[(hh - 1) * NQT]))
                g.dma_start(q_sb[:, (hh % 2) * S:(hh % 2) * S + S],
                            q[hh]).then_inc(s_qh, 16)


        @blk.tensor
        def _(t):
            t.wait_ge(s_in, 16 * 2 * ngrp)
            t.wait_ge(s_ini, 1)
            for ti, (h, qt) in enumerate(tiles):
                gi = h // (nheads // ngrp)
                row = rows[ti]
                ncc = chs[ti]
                if qt == 0:
                    t.wait_ge(s_qh, 16 * (h + 1))
                qsl = q_sb[:, (h % 2) * S + qt * 128: (h % 2) * S + (qt + 1) * 128]
                for c in range(ncc):
                    gc = int(cum_ch[ti]) + c
                    cl = min(512, row - c * 512)
                    if gc >= 2:
                        t.wait_ge(s_p1, gc - 1)
                    bank = gc % 2
                    t.matmul(ps_s[:, bank * 512:bank * 512 + cl], qsl,
                             kt_sb[:, gi * S + c * 512: gi * S + c * 512 + cl],
                             start=True, stop=True).then_inc(s_m1, 1)
                for c in range(ncc):
                    gc = int(cum_ch[ti]) + c
                    cl = min(512, row - c * 512)
                    if gc >= 2:
                        t.wait_ge(s_ex, gc - 1)
                    bank = 2 + gc % 2
                    t.matmul(ps_s[:, bank * 512:bank * 512 + cl], qsl,
                             kt_sb[:, gi * S + c * 512: gi * S + c * 512 + cl],
                             start=True, stop=True).then_inc(s_m2, 1)
                t.wait_ge(s_pm, ti + 1)
                for kti in range(qt + 1):
                    gslot = kti % 16
                    grp = int(cum_gr[ti]) + kti // 8
                    if grp >= 2:
                        t.wait_ge(s_pc, grp - 1)
                    inst = t.transpose(ps_t[:, gslot * 128:(gslot + 1) * 128],
                                       p_sb[:, kti * 128:(kti + 1) * 128], ident[:, :])
                    if kti % 8 == 7 or kti == qt:
                        inst.then_inc(s_tr, 1)
                if ti >= 2:
                    t.wait_ge(s_se, ti - 1)
                for kti in range(qt + 1):
                    grp = int(cum_gr[ti]) + kti // 8
                    if kti % 8 == 0:
                        t.wait_ge(s_pc, grp + 1)
                    inst = t.matmul(ps_o[:, 0:128],
                                    v_sb[:, gi * S + kti * 128: gi * S + (kti + 1) * 128],
                                    pt_sb[:, kti * 128:(kti + 1) * 128],
                                    start=(kti == 0), stop=(kti == qt))
                inst.then_inc(s_pv, 1)

        @blk.vector
        def _(ve):
            for ti, (h, qt) in enumerate(tiles):
                row = rows[ti]
                ncc = chs[ti]
                for c in range(ncc):
                    gc = int(cum_ch[ti]) + c
                    cl = min(512, row - c * 512)
                    bank = gc % 2
                    ve.wait_ge(s_m1, gc + 1)
                    if c == ncc - 1:
                        ve.tensor_add(ps_s[:, bank * 512 + cl - 128: bank * 512 + cl],
                                      ps_s[:, bank * 512 + cl - 128: bank * 512 + cl],
                                      mask[:, :])
                    ve.reduce_max(mx[:, c:c + 1], ps_s[:, bank * 512:bank * 512 + cl],
                                  axis=AX.X).then_inc(s_p1, 1)
                ve.reduce_max(mx[:, 8:9], mx[:, :ncc], axis=AX.X)
                ve.drain()
                ve.tensor_scalar(mx[:, 9:10], mx[:, 8:9], cst[-ISCALE], None,
                                 ALU.mult)
                ve.drain().then_inc(s_ng, 1)
                gc_last = int(cum_ch[ti]) + ncc - 1
                cl = row - (ncc - 1) * 512
                bank = 2 + gc_last % 2
                ve.wait_ge(s_m2, gc_last + 1)
                ve.tensor_add(ps_s[:, bank * 512 + cl - 128: bank * 512 + cl],
                              ps_s[:, bank * 512 + cl - 128: bank * 512 + cl],
                              mask[:, :])
                ve.drain().then_inc(s_dm, 1)
                ve.wait_ge(s_exd, ti + 1)
                ve.reduce_sum(mx[:, 18:19], mx[:, 10:10 + ncc], axis=AX.X)
                ve.drain()
                ve.reciprocal(mx[:, 19:20], mx[:, 18:19])
                ve.drain()
                ve.tensor_scalar_mul(p_sb[:, :rows[ti]], p_sb[:, :rows[ti]],
                                     mx[:, 19:20])
                ve.drain().then_inc(s_pm, 1)
                for gidx in range(grs[ti]):
                    grp = int(cum_gr[ti]) + gidx
                    lo = gidx * 8
                    hi = min(qt + 1, lo + 8)
                    ve.wait_ge(s_tr, grp + 1)
                    ve.tensor_copy(pt_sb[:, lo * 128:hi * 128],
                                   ps_t[:, (lo % 16) * 128:(lo % 16 + hi - lo) * 128])
                    ve.drain().then_inc(s_pc, 1)
                ve.wait_ge(s_pv, ti + 1)
                if h >= 1 and ti % NQT >= 0:
                    ve.wait_ge(s_st, 16 * h)
                ve.tensor_copy(so_sb[:, qt * 128:(qt + 1) * 128],
                               ps_o[:, 0:128])
                ve.drain().then_inc(s_se, 1)

        @blk.scalar
        def _(sc):
            for ti, (h, qt) in enumerate(tiles):
                row = rows[ti]
                ncc = chs[ti]
                sc.wait_ge(s_ng, ti + 1)
                if ti >= 1:
                    sc.wait_ge(s_tr, int(cum_gr[ti]))
                for c in range(ncc):
                    gc = int(cum_ch[ti]) + c
                    cl = min(512, row - c * 512)
                    bank = 2 + gc % 2
                    sc.wait_ge(s_m2, gc + 1)
                    if c == ncc - 1:
                        sc.wait_ge(s_dm, ti + 1)
                    sc.activation(p_sb[:, c * 512:c * 512 + cl],
                                  ps_s[:, bank * 512:bank * 512 + cl],
                                  AF.Exp, bias=mx[:, 9:10], scale=ISCALE,
                                  accum_out=mx[:, 10 + c:11 + c]).then_inc(s_ex, 1)
                sc.drain().then_inc(s_exd, 1)

        @blk.sync
        def _(sy):
            for h in range(nheads):
                sy.wait_ge(s_se, (h + 1) * NQT)
                sy.dma_start(s_out[h * 128:(h + 1) * 128, :NQT * 128],
                             so_sb[:, :NQT * 128]).then_inc(s_st, 16)
            sy.wait_ge(s_st, 16 * nheads)
            sy.wait_ge(s_p1, int(cum_ch[-1]))
            sy.wait_ge(s_ex, int(cum_ch[-1]))
    return nc if ncv is None else None


def build_ffn0(ncv=None, io=None, T=TPC):
    if ncv is None:
        nc = bass.Bass("TRN2", target_bir_lowering=False, detect_race_conditions=False)
        xn = nc.dram_tensor("xn", [D, T], F32, kind="ExternalInput")
        xr = nc.dram_tensor("xr", [D, T], F32, kind="ExternalInput")
        sr = nc.dram_tensor("sr", [D, T], F32, kind="ExternalInput")
        wg = nc.dram_tensor("wg", [FF0 // 128, D, 128], F32, kind="ExternalInput")
        wu = nc.dram_tensor("wu", [FF0 // 128, D, 128], F32, kind="ExternalInput")
        wd = nc.dram_tensor("wd", [D // 128, FF0, 128], F32, kind="ExternalInput")
        x1 = nc.dram_tensor("x1", [D, T], F32, kind="ExternalOutput")
    else:
        nc = ncv
        xn, xr, sr, wg, wu, wd, x1 = (io[n] for n in (
            "xn", "xr", "sr", "wg", "wu", "wd", "x1"))

    MCH = FF0 // 128
    MD = D // 128
    KH = FF0 // 128

    with ExitStack() as ex:
        xn_sb = ex.enter_context(nc.sbuf_tensor("xn_sb", [128, KC * 512], BF16))
        gat = ex.enter_context(nc.sbuf_tensor("gat", [128, MCH * 512], BF16))
        w_sb = ex.enter_context(nc.sbuf_tensor("w_sb", [128, 4 * KC * 128], BF16))
        wd_sb = ex.enter_context(nc.sbuf_tensor("wd_sb", [128, 2 * KH * 128], BF16))
        t1 = ex.enter_context(nc.sbuf_tensor("t1", [128, 2 * 512], F32))
        g1 = ex.enter_context(nc.sbuf_tensor("g1", [128, 2 * 512], F32))
        xs_sb = ex.enter_context(nc.sbuf_tensor("xs_sb", [128, 4 * 512], F32))
        ss_sb = ex.enter_context(nc.sbuf_tensor("ss_sb", [128, 4 * 512], F32))
        o_sb = ex.enter_context(nc.sbuf_tensor("o_sb", [128, 2 * 512], F32))
        ps = ex.enter_context(nc.psum_tensor("ps", [128, 8 * 512], F32))
        if io is None:
            dmy, ps_wrm = warmup(nc, ex, ps[:, 0:128])
        else:
            dmy, ps_wrm = io["dmy"], io["ps_wrm"]
        s_xn = ex.enter_context(nc.semaphore("s_xn"))
        s_w = ex.enter_context(nc.semaphore("s_w"))
        s_mm = ex.enter_context(nc.semaphore("s_mm"))
        s_si = ex.enter_context(nc.semaphore("s_si"))
        s_gt = ex.enter_context(nc.semaphore("s_gt"))
        s_wd = ex.enter_context(nc.semaphore("s_wd"))
        s_m2 = ex.enter_context(nc.semaphore("s_m2"))
        s_g1 = ex.enter_context(nc.semaphore("s_g1"))
        s_xs = ex.enter_context(nc.semaphore("s_xs"))
        s_o = ex.enter_context(nc.semaphore("s_o"))
        s_so = ex.enter_context(nc.semaphore("s_so"))

        for tc in range(T // 512):
            tsl = slice(tc * 512, (tc + 1) * 512)
            bw, bmm, bsi, bgt = (16 * (2 * MCH + 1) * tc, 2 * MCH * tc,
                                 MCH * tc, MCH * tc)
            bwd, bm2, bg1 = 16 * (MD + 1) * tc, MD * tc, MD * tc
            bxs, bo, bso = 32 * MD * tc, MD * tc, 16 * MD * tc

            bxn2 = 16 * (KC + 1) * tc
            with nc.Block() as bA:
                @bA.gpsimd
                def _(g):
                    for kc in range(KC):
                        fm_load(g, xn_sb, kc * 512,
                                xn[kc * 128:(kc + 1) * 128, tsl], 512, s_xn)
                    g.dma_start(dmy[:, :], xn[0:128, 0:128]).then_inc(s_xn, 16)
                    for mc in range(MCH):
                        for wi, wsrc in ((0, wg), (1, wu)):
                            j = mc * 2 + wi
                            if j >= 4:
                                g.wait_ge(s_mm, bmm + j - 3)
                            buf = j % 4
                            g.dma_start(
                                w_sb[:, buf * KC * 128:(buf + 1) * KC * 128]
                                .rearrange("p (kc m) -> p kc m", m=128),
                                wsrc[mc].rearrange("(kc p) m -> p kc m", p=128),
                            ).then_inc(s_w, 16)
                    g.dma_start(dmy[:, :], xn[0:128, 0:128]).then_inc(s_w, 16)

                @bA.tensor
                def _(t):
                    t.wait_ge(s_xn, bxn2 + 16 * (KC + 1))
                    for mc in range(MCH):
                        for wi in range(2):
                            j = mc * 2 + wi
                            t.wait_ge(s_w, bw + 16 * (j + 2))
                            bank = j % 4
                            if mc >= 2:
                                if wi == 0:
                                    t.wait_ge(s_si, bsi + mc - 1)
                                else:
                                    t.wait_ge(s_gt, bgt + mc - 1)
                            buf = j % 4
                            for kc in range(KC):
                                inst = t.matmul(
                                    ps[:, bank * 512:(bank + 1) * 512],
                                    w_sb[:, buf * KC * 128 + kc * 128:
                                         buf * KC * 128 + (kc + 1) * 128],
                                    xn_sb[:, kc * 512:(kc + 1) * 512],
                                    start=(kc == 0), stop=(kc == KC - 1))
                            inst.then_inc(s_mm, 1)

                @bA.scalar
                def _(sc):
                    for mc in range(MCH):
                        bank = (mc * 2) % 4
                        sc.wait_ge(s_mm, bmm + mc * 2 + 1)
                        if mc >= 2:
                            sc.wait_ge(s_gt, bgt + mc - 1)
                        sc.activation(t1[:, (mc % 2) * 512:(mc % 2 + 1) * 512],
                                      ps[:, bank * 512:(bank + 1) * 512], AF.Silu)
                        sc.drain().then_inc(s_si, 1)

                @bA.vector
                def _(ve):
                    for mc in range(MCH):
                        bank = (mc * 2 + 1) % 4
                        ve.wait_ge(s_mm, bmm + mc * 2 + 2)
                        ve.wait_ge(s_si, bsi + mc + 1)
                        ve.tensor_mul(gat[:, mc * 512:(mc + 1) * 512],
                                      t1[:, (mc % 2) * 512:(mc % 2 + 1) * 512],
                                      ps[:, bank * 512:(bank + 1) * 512]
                                      ).then_inc(s_gt, 1)
                    if tc == T // 512 - 1:
                        ve.wait_ge(s_gt, bgt + MCH)

            with nc.Block() as bB:
                @bB.gpsimd
                def _(g):
                    for md in range(MD):
                        if md >= 2:
                            g.wait_ge(s_m2, bm2 + md - 1)
                        buf = md % 2
                        g.dma_start(
                            wd_sb[:, buf * KH * 128:(buf + 1) * KH * 128]
                            .rearrange("p (kc m) -> p kc m", m=128),
                            wd[md].rearrange("(kc p) m -> p kc m", p=128),
                        ).then_inc(s_wd, 16)
                    g.dma_start(dmy[:, :], xn[0:128, 0:128]).then_inc(s_wd, 16)

                @bB.sync
                def _(sy):
                    for md in range(MD + 1):
                        if md < MD:
                            if md >= 4:
                                sy.wait_ge(s_o, bo + md - 3)
                            b4 = md % 4
                            sy.dma_start(xs_sb[:, b4 * 512:(b4 + 1) * 512],
                                         xr[md * 128:(md + 1) * 128, tsl]
                                         ).then_inc(s_xs, 16)
                            sy.dma_start(ss_sb[:, b4 * 512:(b4 + 1) * 512],
                                         sr[md * 128:(md + 1) * 128, tsl]
                                         ).then_inc(s_xs, 16)
                        if md >= 1:
                            sy.wait_ge(s_o, bo + md)
                            b2 = (md - 1) % 2
                            sy.dma_start(x1[(md - 1) * 128:md * 128, tsl],
                                         o_sb[:, b2 * 512:(b2 + 1) * 512]
                                         ).then_inc(s_so, 16)
                    sy.wait_ge(s_so, bso + 16 * MD)

                @bB.tensor
                def _(t):
                    for md in range(MD):
                        t.wait_ge(s_wd, bwd + 16 * (md + 2))
                        buf = md % 2
                        bank = 4 + md % 4
                        if md >= 4:
                            t.wait_ge(s_g1, bg1 + md - 3)
                        for kh in range(KH):
                            inst = t.matmul(
                                ps[:, bank * 512:(bank + 1) * 512],
                                wd_sb[:, buf * KH * 128 + kh * 128:
                                      buf * KH * 128 + (kh + 1) * 128],
                                gat[:, kh * 512:(kh + 1) * 512],
                                start=(kh == 0), stop=(kh == KH - 1))
                        inst.then_inc(s_m2, 1)

                @bB.scalar
                def _(sc):
                    for md in range(MD):
                        bank = 4 + md % 4
                        sc.wait_ge(s_m2, bm2 + md + 1)
                        if md >= 2:
                            sc.wait_ge(s_o, bo + md - 1)
                        sc.activation(g1[:, (md % 2) * 512:(md % 2 + 1) * 512],
                                      ps[:, bank * 512:(bank + 1) * 512], AF.Silu)
                        sc.drain().then_inc(s_g1, 1)

                @bB.vector
                def _(ve):
                    for md in range(MD):
                        b4 = md % 4
                        b2 = md % 2
                        ve.wait_ge(s_g1, bg1 + md + 1)
                        ve.wait_ge(s_xs, bxs + 32 * (md + 1))
                        if md >= 2:
                            ve.wait_ge(s_so, bso + 16 * (md - 1))
                        ve.tensor_mul(o_sb[:, b2 * 512:(b2 + 1) * 512],
                                      g1[:, b2 * 512:(b2 + 1) * 512],
                                      ss_sb[:, b4 * 512:(b4 + 1) * 512])
                        ve.tensor_add(o_sb[:, b2 * 512:(b2 + 1) * 512],
                                      o_sb[:, b2 * 512:(b2 + 1) * 512],
                                      xs_sb[:, b4 * 512:(b4 + 1) * 512])
                        ve.drain().then_inc(s_o, 1)
    return nc if ncv is None else None


def build_scan(ncv=None, io=None, nem=8):
    if ncv is None:
        nc = bass.Bass("TRN2", target_bir_lowering=False, detect_race_conditions=False)
        xn = nc.dram_tensor("xn", [D, S], F32, kind="ExternalInput")
        wi = nc.dram_tensor("wi", [nem, D, 128], F32, kind="ExternalInput")
        wf = nc.dram_tensor("wf", [nem, D, 128], F32, kind="ExternalInput")
        wg = nc.dram_tensor("wg", [nem, D, 128], F32, kind="ExternalInput")
        xg = nc.dram_tensor("xg", [nem * 128, S], F32, kind="ExternalOutput")
    else:
        nc = ncv
        xn, wi, wf, wg, xg = (io[n] for n in ("xn", "wi", "wf", "wg", "xg"))
    NTC = S // 512

    with ExitStack() as ex:
        xn_sb = ex.enter_context(nc.sbuf_tensor("xn_sb", [128, KC * S], BF16))
        w_sb = ex.enter_context(nc.sbuf_tensor("w_sb", [128, 6 * KC * 128], BF16))
        a_sb = ex.enter_context(nc.sbuf_tensor("a_sb", [128, 2 * 512], F32))
        si_sb = ex.enter_context(nc.sbuf_tensor("si_sb", [128, 2 * 512], F32))
        sg_sb = ex.enter_context(nc.sbuf_tensor("sg_sb", [128, 2 * 512], F32))
        b_sb = ex.enter_context(nc.sbuf_tensor("b_sb", [128, 512], F32))
        h_sb = ex.enter_context(nc.sbuf_tensor("h_sb", [128, 512], F32))
        xg_sb = ex.enter_context(nc.sbuf_tensor("xg_sb", [128, 2 * 512], F32))
        hst = ex.enter_context(nc.sbuf_tensor("hst", [128, nem], F32))
        ps = ex.enter_context(nc.psum_tensor("ps", [128, 8 * 512], F32))
        s_xn = ex.enter_context(nc.semaphore("s_xn"))
        s_w = ex.enter_context(nc.semaphore("s_w"))
        s_mm = ex.enter_context(nc.semaphore("s_mm"))
        s_ac = ex.enter_context(nc.semaphore("s_ac"))
        s_dv = ex.enter_context(nc.semaphore("s_dv"))
        s_st = ex.enter_context(nc.semaphore("s_st"))
        s_ini = ex.enter_context(nc.semaphore("s_ini"))
        if io is None:
            dmy, ps_wrm = warmup(nc, ex, ps[:, 0:128])
        else:
            dmy, ps_wrm = io["dmy"], io["ps_wrm"]
        blk = ex.enter_context(nc.Block())

        @blk.gpsimd
        def _(g):
            g.memset(hst[:, :], 0.0)
            g.engine_nop().then_inc(s_ini, 1)
            for kc in range(KC):
                fm_load(g, xn_sb, kc * S, xn[kc * 128:(kc + 1) * 128, :], S, s_xn)
            g.dma_start(dmy[:, :], xn[0:128, 0:128]).then_inc(s_xn, 16)
            for em in range(nem):
                if em >= 2:
                    g.wait_ge(s_mm, 3 * NTC * (em - 1))
                buf = em % 2
                for wix, wsrc in ((0, wi), (1, wf), (2, wg)):
                    g.dma_start(
                        w_sb[:, (buf * 3 + wix) * KC * 128:
                             (buf * 3 + wix + 1) * KC * 128]
                        .rearrange("p (kc m) -> p kc m", m=128),
                        wsrc[em].rearrange("(kc p) m -> p kc m", p=128),
                    ).then_inc(s_w, 16)
            g.dma_start(dmy[:, :], xn[0:128, 0:128]).then_inc(s_w, 16)

        @blk.tensor
        def _(t):
            t.wait_ge(s_xn, 16 * (KC + 1))
            t.wait_ge(s_ini, 1)
            for em in range(nem):
                t.wait_ge(s_w, 16 * (3 * (em + 1) + 1))
                buf = em % 2
                for tc in range(NTC):
                    j = em * NTC + tc
                    if j >= 2:
                        t.wait_ge(s_ac, j - 1)
                    pb = (j % 2) * 3
                    for wix in range(3):
                        wb = (buf * 3 + wix) * KC * 128
                        for kc in range(KC):
                            inst = t.matmul(
                                ps[:, (pb + wix) * 512:(pb + wix + 1) * 512],
                                w_sb[:, wb + kc * 128:wb + (kc + 1) * 128],
                                xn_sb[:, kc * S + tc * 512:kc * S + (tc + 1) * 512],
                                start=(kc == 0), stop=(kc == KC - 1))
                        inst.then_inc(s_mm, 1)

        @blk.scalar
        def _(sc):
            for em in range(nem):
                for tc in range(NTC):
                    j = em * NTC + tc
                    pb = (j % 2) * 3
                    par = j % 2
                    sc.wait_ge(s_mm, 3 * (j + 1))
                    if j >= 2:
                        sc.wait_ge(s_dv, j - 1)
                    sc.activation(si_sb[:, par * 512:(par + 1) * 512],
                                  ps[:, pb * 512:(pb + 1) * 512], AF.Silu)
                    sc.activation(a_sb[:, par * 512:(par + 1) * 512],
                                  ps[:, (pb + 1) * 512:(pb + 2) * 512], AF.Sigmoid)
                    sc.activation(sg_sb[:, par * 512:(par + 1) * 512],
                                  ps[:, (pb + 2) * 512:(pb + 3) * 512], AF.Silu)
                    sc.drain().then_inc(s_ac, 1)

        @blk.vector
        def _(ve):
            for em in range(nem):
                for tc in range(NTC):
                    j = em * NTC + tc
                    par = j % 2
                    asl = a_sb[:, par * 512:(par + 1) * 512]
                    sisl = si_sb[:, par * 512:(par + 1) * 512]
                    sgsl = sg_sb[:, par * 512:(par + 1) * 512]
                    ve.wait_ge(s_ac, j + 1)
                    ve.tensor_mul(b_sb[:, :], sisl, asl)
                    ve.tensor_sub(b_sb[:, :], sisl, b_sb[:, :])
                    ve.drain()
                    ve.tensor_tensor_scan(h_sb[:, :], asl, b_sb[:, :],
                                          hst[:, em:em + 1], op0=ALU.mult,
                                          op1=ALU.add)
                    ve.drain()
                    ve.tensor_copy(hst[:, em:em + 1], h_sb[:, 511:512])
                    if j >= 2:
                        ve.wait_ge(s_st, 16 * (j - 1))
                    ve.tensor_mul(xg_sb[:, par * 512:(par + 1) * 512],
                                  h_sb[:, :], sgsl)
                    ve.drain().then_inc(s_dv, 1)

        @blk.sync
        def _(sy):
            for em in range(nem):
                for tc in range(NTC):
                    j = em * NTC + tc
                    par = j % 2
                    sy.wait_ge(s_dv, j + 1)
                    sy.dma_start(xg[em * 128:(em + 1) * 128, tc * 512:(tc + 1) * 512],
                                 xg_sb[:, par * 512:(par + 1) * 512]).then_inc(s_st, 16)
            sy.wait_ge(s_st, 16 * nem * NTC)
    return nc if ncv is None else None


def build_rms(ncv=None, io=None):
    T = TPC
    if ncv is None:
        nc = bass.Bass("TRN2", target_bir_lowering=False, detect_race_conditions=False)
        xg = nc.dram_tensor("xg", [T, D], F32, kind="ExternalInput")
        wn = nc.dram_tensor("wn", [1, D], F32, kind="ExternalInput")
        y = nc.dram_tensor("y", [T, D], BF16, kind="ExternalOutput")
    else:
        nc = ncv
        xg, wn, y = io["xg"], io["wn"], io["y"]
    NT = T // 128

    with ExitStack() as ex:
        xt = ex.enter_context(nc.sbuf_tensor("xt", [128, 2 * D], F32))
        sq = ex.enter_context(nc.sbuf_tensor("sq", [128, D], F32))
        on = ex.enter_context(nc.sbuf_tensor("on", [128, 2 * D], BF16))
        wnr = ex.enter_context(nc.sbuf_tensor("wnr", [1, D], F32))
        wnb = ex.enter_context(nc.sbuf_tensor("wnb", [128, D], F32))
        ones = ex.enter_context(nc.sbuf_tensor("ones", [1, 128], F32))
        st = ex.enter_context(nc.sbuf_tensor("st", [128, 8], F32))
        ps = ex.enter_context(nc.psum_tensor("ps", [128, 4 * 512], F32))
        s_ld = ex.enter_context(nc.semaphore("s_ld"))
        s_b = ex.enter_context(nc.semaphore("s_b"))
        s_ac = ex.enter_context(nc.semaphore("s_ac"))
        s_dv = ex.enter_context(nc.semaphore("s_dv"))
        s_st = ex.enter_context(nc.semaphore("s_st"))
        s_mm = ex.enter_context(nc.semaphore("s_mm"))
        cst = setup_consts(nc, ex, [EPS])

        with nc.Block() as b0:
            @b0.gpsimd
            def _(g):
                g.memset(ones[:, :], 1.0)
                g.dma_start(wnr[:, :], wn[:, :]).then_inc(s_b, 16)

            @b0.tensor
            def _(t):
                t.wait_ge(s_b, 16)
                for c in range(4):
                    t.matmul(ps[:, c * 512:(c + 1) * 512], ones[:, :],
                             wnr[:, c * 512:(c + 1) * 512],
                             start=True, stop=True).then_inc(s_mm, 1)

            @b0.vector
            def _(ve):
                ve.wait_ge(s_mm, 4)
                ve.tensor_copy(wnb[:, :], ps[:, :D])

        with nc.Block() as b1:
            @b1.sync
            def _(sy):
                for i in range(NT):
                    if i >= 2:
                        sy.wait_ge(s_dv, i - 1)
                    b = i % 2
                    sy.dma_start(xt[:, b * D:(b + 1) * D],
                                 xg[i * 128:(i + 1) * 128, :]).then_inc(s_ld, 16)

            @b1.scalar
            def _(sc):
                for i in range(NT):
                    b = i % 2
                    o = b * 4
                    sc.wait_ge(s_ld, 16 * (i + 1))
                    sc.activation(sq[:, :], xt[:, b * D:(b + 1) * D], AF.Square,
                                  accum_out=st[:, o:o + 1])
                    sc.drain()
                    sc.activation(st[:, o + 1:o + 2], st[:, o:o + 1], AF.Sqrt,
                                  scale=1.0 / D, bias=cst[EPS])
                    sc.drain().then_inc(s_ac, 1)

            @b1.vector
            def _(ve):
                for i in range(NT):
                    b = i % 2
                    o = b * 4
                    ve.wait_ge(s_ac, i + 1)
                    ve.reciprocal(st[:, o + 2:o + 3], st[:, o + 1:o + 2])
                    ve.drain()
                    ve.tensor_scalar_mul(xt[:, b * D:(b + 1) * D],
                                         xt[:, b * D:(b + 1) * D], st[:, o + 2:o + 3])
                    if i >= 2:
                        ve.wait_ge(s_st, 16 * (i - 1))
                    ve.tensor_mul(on[:, b * D:(b + 1) * D],
                                  xt[:, b * D:(b + 1) * D], wnb[:, :])
                    ve.drain().then_inc(s_dv, 1)

            @b1.gpsimd
            def _(g):
                for i in range(NT):
                    b = i % 2
                    g.wait_ge(s_dv, i + 1)
                    g.dma_start(y[i * 128:(i + 1) * 128, :],
                                on[:, b * D:(b + 1) * D]).then_inc(s_st, 16)
                g.wait_ge(s_st, 16 * NT)
    return nc if ncv is None else None


def build_wo(ncv=None, io=None):
    T = TPC
    if ncv is None:
        nc = bass.Bass("TRN2", target_bir_lowering=False, detect_race_conditions=False)
        yf = nc.dram_tensor("yf", [D, T], BF16, kind="ExternalInput")
        wo = nc.dram_tensor("wo", [KC, D, 128], F32, kind="ExternalInput")
        s1 = nc.dram_tensor("s1", [D, T], F32, kind="ExternalOutput")
    else:
        nc = ncv
        yf, wo, s1 = io["yf"], io["wo"], io["s1"]
    TC = T // 512
    MD = D // 128

    with ExitStack() as ex:
        y_sb = ex.enter_context(nc.sbuf_tensor("y_sb", [128, KC * T], BF16))
        w_sb = ex.enter_context(nc.sbuf_tensor("w_sb", [128, 2 * KC * 128], BF16))
        o_sb = ex.enter_context(nc.sbuf_tensor("o_sb", [128, 4 * 512], F32))
        ps = ex.enter_context(nc.psum_tensor("ps", [128, 8 * 512], F32))
        if io is None:
            dmy, ps_wrm = warmup(nc, ex, ps[:, 0:128])
        else:
            dmy, ps_wrm = io["dmy"], io["ps_wrm"]
        s_y = ex.enter_context(nc.semaphore("s_y"))
        s_w = ex.enter_context(nc.semaphore("s_w"))
        s_mm = ex.enter_context(nc.semaphore("s_mm"))
        s_ev = ex.enter_context(nc.semaphore("s_ev"))
        s_st = ex.enter_context(nc.semaphore("s_st"))
        blk = ex.enter_context(nc.Block())

        @blk.sync
        def _(sy):
            for kc in range(KC):
                sy.dma_start(y_sb[:, kc * T:(kc + 1) * T],
                             yf[kc * 128:(kc + 1) * 128, :]).then_inc(s_y, 16)

        @blk.gpsimd
        def _(g):
            for md in range(MD):
                if md >= 2:
                    g.wait_ge(s_mm, (md - 1) * TC)
                buf = md % 2
                g.dma_start(
                    w_sb[:, buf * KC * 128:(buf + 1) * KC * 128]
                    .rearrange("p (kc m) -> p kc m", m=128),
                    wo[md].rearrange("(kc p) m -> p kc m", p=128),
                ).then_inc(s_w, 16)
            g.dma_start(dmy[:, :], bass.AP(wo, 0, [[128, 128], [1, 128]])
                        ).then_inc(s_w, 16)

        @blk.tensor
        def _(t):
            t.wait_ge(s_y, 16 * KC)
            cnt = 0
            for md in range(MD):
                t.wait_ge(s_w, 16 * (md + 2))
                buf = md % 2
                for tc in range(TC):
                    bank = cnt % 8
                    if cnt >= 8:
                        t.wait_ge(s_ev, cnt - 7)
                    for kc in range(KC):
                        inst = t.matmul(
                            ps[:, bank * 512:(bank + 1) * 512],
                            w_sb[:, buf * KC * 128 + kc * 128:
                                 buf * KC * 128 + (kc + 1) * 128],
                            y_sb[:, kc * T + tc * 512:kc * T + (tc + 1) * 512],
                            start=(kc == 0), stop=(kc == KC - 1))
                    inst.then_inc(s_mm, 1)
                    cnt += 1

        @blk.vector
        def _(ve):
            cnt = 0
            for md in range(MD):
                for tc in range(TC):
                    bank = cnt % 8
                    ob = cnt % 4
                    ve.wait_ge(s_mm, cnt + 1)
                    if cnt >= 4:
                        ve.wait_ge(s_st, 16 * (cnt - 3))
                    ve.tensor_copy(o_sb[:, ob * 512:(ob + 1) * 512],
                                   ps[:, bank * 512:(bank + 1) * 512])
                    ve.drain().then_inc(s_ev, 1)
                    cnt += 1

        @blk.scalar
        def _(sc):
            cnt = 0
            for md in range(MD):
                for tc in range(TC):
                    ob = cnt % 4
                    sc.wait_ge(s_ev, cnt + 1)
                    sc.dma_start(s1[md * 128:(md + 1) * 128, tc * 512:(tc + 1) * 512],
                                 o_sb[:, ob * 512:(ob + 1) * 512]).then_inc(s_st, 16)
                    cnt += 1
            sc.wait_ge(s_st, 16 * MD * TC)
    return nc if ncv is None else None


def build_moe(ncv=None, io=None):
    T = TPC
    TH = 1024
    NSRC = 10
    if ncv is None:
        nc = bass.Bass("TRN2", target_bir_lowering=False, detect_race_conditions=False)
        xn = nc.dram_tensor("xn", [D, T], F32, kind="ExternalInput")
        x1 = nc.dram_tensor("x1", [D, T], F32, kind="ExternalInput")
        s1 = nc.dram_tensor("s1", [D, T], F32, kind="ExternalInput")
        gw = nc.dram_tensor("gw", [KC, 128, 8], F32, kind="ExternalInput")
        w1 = nc.dram_tensor("w1", [NSRC, FR // 128, D, 128], F32, kind="ExternalInput")
        w3 = nc.dram_tensor("w3", [NSRC, FR // 128, D, 128], F32, kind="ExternalInput")
        w2 = nc.dram_tensor("w2", [NSRC, D // 128, FR, 128], F32, kind="ExternalInput")
        out = nc.dram_tensor("out", [D, T], F32, kind="ExternalOutput")
        cw_dbg = nc.dram_tensor("cw_dbg", [NE, T], F32, kind="ExternalOutput")
        cwt_d = nc.dram_tensor("cwt_d", [NE, T], F32)
    else:
        nc = ncv
        xn, x1, s1, gw, w1, w3, w2, out, cw_dbg, cwt_d = (io[n] for n in (
            "xn", "x1", "s1", "gw", "w1", "w3", "w2", "out", "cw_dbg", "cwt_d"))

    MR = FR // 128
    MD = D // 128
    NTT = TH // 128

    with ExitStack() as ex:
        xn_sb = ex.enter_context(nc.sbuf_tensor("xn_sb", [128, KC * TH], BF16))
        gw_sb = ex.enter_context(nc.sbuf_tensor("gw_sb", [128, KC * 8], BF16))
        idf = ex.enter_context(nc.sbuf_tensor("idf", [128, 128], F32))
        lgt = ex.enter_context(nc.sbuf_tensor("lgt", [128, 64], F32))
        lgs = ex.enter_context(nc.sbuf_tensor("lgs", [128, 64], F32))
        p8 = ex.enter_context(nc.sbuf_tensor("p8", [128, 64], F32))
        smn = ex.enter_context(nc.sbuf_tensor("smn", [128, 64], F32))
        mskg = ex.enter_context(nc.sbuf_tensor("mskg", [128, 64], F32))
        cw_all = ex.enter_context(nc.sbuf_tensor("cw_all", [128, 64], F32))
        cw8 = ex.enter_context(nc.sbuf_tensor("cw8", [8, 1024], F32))
        gst = ex.enter_context(nc.sbuf_tensor("gst", [128, 40], F32))
        geb = ex.enter_context(nc.sbuf_tensor("geb", [128, MR * TH], BF16))
        acc = ex.enter_context(nc.sbuf_tensor("acc", [128, MD * TH], F32))
        w_sb = ex.enter_context(nc.sbuf_tensor("w_sb", [128, 4 * KC * 128], BF16))
        wd_sb = ex.enter_context(nc.sbuf_tensor("wd_sb", [128, 2 * MR * 128], BF16))
        t1 = ex.enter_context(nc.sbuf_tensor("t1", [128, 2 * 512], F32))
        t2 = ex.enter_context(nc.sbuf_tensor("t2", [128, 2 * 512], F32))
        cwb = ex.enter_context(nc.sbuf_tensor("cwb", [128, 2 * 512], F32))
        g1s = ex.enter_context(nc.sbuf_tensor("g1s", [128, 2 * 512], F32))
        xs_sb = ex.enter_context(nc.sbuf_tensor("xs_sb", [128, 2 * 512], F32))
        ss_sb = ex.enter_context(nc.sbuf_tensor("ss_sb", [128, 2 * 512], F32))
        o_sb = ex.enter_context(nc.sbuf_tensor("o_sb", [128, 2 * 512], F32))
        ps = ex.enter_context(nc.psum_tensor("ps", [128, 6 * 512], F32))
        ps_g = ex.enter_context(nc.psum_tensor("ps_g", [128, 2 * 512], F32))
        s_k = ex.enter_context(nc.semaphore("s_k"))
        s_xn = ex.enter_context(nc.semaphore("s_xn"))
        s_gm = ex.enter_context(nc.semaphore("s_gm"))
        s_gcp = ex.enter_context(nc.semaphore("s_gcp"))
        s_gd1 = ex.enter_context(nc.semaphore("s_gd1"))
        s_ge = ex.enter_context(nc.semaphore("s_ge"))
        s_ged = ex.enter_context(nc.semaphore("s_ged"))
        s_gd2 = ex.enter_context(nc.semaphore("s_gd2"))
        s_gt = ex.enter_context(nc.semaphore("s_gt"))
        s_gc = ex.enter_context(nc.semaphore("s_gc"))
        s_gw = ex.enter_context(nc.semaphore("s_gw"))
        s_w = ex.enter_context(nc.semaphore("s_w"))
        s_mm = ex.enter_context(nc.semaphore("s_mm"))
        s_si = ex.enter_context(nc.semaphore("s_si"))
        s_t2 = ex.enter_context(nc.semaphore("s_t2"))
        s_cwb = ex.enter_context(nc.semaphore("s_cwb"))
        s_wd = ex.enter_context(nc.semaphore("s_wd"))
        s_m2 = ex.enter_context(nc.semaphore("s_m2"))
        s_ad = ex.enter_context(nc.semaphore("s_ad"))
        s_g1 = ex.enter_context(nc.semaphore("s_g1"))
        s_xs = ex.enter_context(nc.semaphore("s_xs"))
        s_o = ex.enter_context(nc.semaphore("s_o"))
        s_so = ex.enter_context(nc.semaphore("s_so"))
        if io is None:
            dmy, ps_wrm = warmup(nc, ex, ps[:, 0:128])
            cst = setup_consts(nc, ex, [-1.0])
        else:
            dmy, ps_wrm, cst = io["dmy"], io["ps_wrm"], io["cst"]

        with nc.Block() as bc:
            @bc.gpsimd
            def _(g):
                g.memset(idf[:, :], 0.0)
                g.affine_select(out=idf[:, :], in_=idf[:, :],
                                compare_op=ALU.not_equal, fill=1.0, base=0,
                                pattern=[[-1, 128]], channel_multiplier=1)
                g.dma_start(gw_sb[:, :].rearrange("p (kc e) -> p kc e", e=8),
                            gw.rearrange("kc p e -> p kc e")).then_inc(s_k, 16)
                g.dma_start(dmy[:, :], xn[0:128, 0:128]).then_inc(s_k, 16)

            @bc.tensor
            def _(t):
                t.wait_ge(s_k, 32)

        for th in range(2):
            toff = th * TH
            bxn = 16 * (KC + 1) * th
            bgm = NTT * th
            bgm2 = (NTT + 1) * th
            bgt = NTT * th
            bgt2 = (NTT + 1) * th

            with nc.Block() as bL:
                @bL.gpsimd
                def _(g):
                    for kc in range(KC):
                        fm_load(g, xn_sb, kc * TH,
                                xn[kc * 128:(kc + 1) * 128, toff:toff + TH], TH, s_xn)
                    g.dma_start(dmy[:, :], xn[0:128, 0:128]).then_inc(s_xn, 16)

                @bL.tensor
                def _(t):
                    t.wait_ge(s_xn, bxn + 16 * (KC + 1))

            with nc.Block() as bG:
                @bG.tensor
                def _(t):
                    for tt in range(NTT):
                        if tt >= 2:
                            t.wait_ge(s_gcp, bgm + tt - 1)
                        for kc in range(KC):
                            inst = t.matmul(
                                ps_g[:, (tt % 2) * 512:(tt % 2) * 512 + 8],
                                xn_sb[:, kc * TH + tt * 128:kc * TH + (tt + 1) * 128],
                                gw_sb[:, kc * 8:(kc + 1) * 8],
                                start=(kc == 0), stop=(kc == KC - 1))
                        inst.then_inc(s_gm, 1)
                    t.matmul(ps[:, 0:128], dmy[:, :], dmy[:, :], start=True,
                             stop=True).then_inc(s_gm, 1)
                    t.wait_ge(s_gd2, th + 1)
                    for tt in range(NTT):
                        if tt >= 4:
                            t.wait_ge(s_gc, bgt + tt - 3)
                        t.transpose(ps[0:8, (tt % 4) * 512:(tt % 4) * 512 + 128],
                                    cw_all[:, tt * 8:(tt + 1) * 8],
                                    idf[:, :]).then_inc(s_gt, 1)
                    t.wait_ge(s_gc, bgt + NTT - 3)
                    t.transpose(ps[0:8, 0:128], cw_all[:, 0:8],
                                idf[:, :]).then_inc(s_gt, 1)

                @bG.vector
                def _(ve):
                    for tt in range(NTT):
                        ve.wait_ge(s_gm, bgm2 + tt + 2)
                        ve.tensor_copy(lgt[:, tt * 8:(tt + 1) * 8],
                                       ps_g[:, (tt % 2) * 512:(tt % 2) * 512 + 8]
                                       ).then_inc(s_gcp, 1)
                    g3 = lambda tile: tile[:, :].rearrange("p (g e) -> p g e", e=8)
                    def bc(col):
                        return bass.AP(gst.ap().tensor, col * 1,
                                       [[40, 128], [1, 8], [0, 8]])
                    ve.reduce_max(gst[:, 0:8], g3(lgt), axis=AX.X)
                    ve.drain()
                    ve.tensor_tensor(g3(lgs), g3(lgt), bc(0), ALU.subtract)
                    ve.drain().then_inc(s_gd1, 1)
                    ve.wait_ge(s_ged, th + 1)
                    ve.reduce_sum(gst[:, 8:16], g3(p8), axis=AX.X)
                    ve.drain()
                    ve.reciprocal(gst[:, 16:24], gst[:, 8:16])
                    ve.drain()
                    ve.tensor_tensor(g3(smn), g3(p8), bc(16), ALU.mult)
                    ve.reduce_max(gst[:, 24:32], g3(smn), axis=AX.X)
                    ve.drain()
                    ve.tensor_tensor(g3(mskg), g3(smn), bc(24), ALU.is_lt)
                    ve.tensor_mul(mskg[:, :], smn[:, :], mskg[:, :])
                    ve.reduce_max(gst[:, 32:40], g3(mskg), axis=AX.X)
                    ve.drain()
                    ve.tensor_tensor(g3(mskg), g3(smn), bc(32), ALU.is_ge)
                    ve.tensor_mul(cw_all[:, :], smn[:, :], mskg[:, :])
                    ve.drain().then_inc(s_gd2, 1)
                    for tt in range(NTT):
                        ve.wait_ge(s_gt, bgt2 + tt + 2)
                        ve.tensor_copy(cw8[0:8, tt * 128:(tt + 1) * 128],
                                       ps[0:8, (tt % 4) * 512:(tt % 4) * 512 + 128])
                        ve.drain().then_inc(s_gc, 1)

                @bG.scalar
                def _(sc):
                    sc.wait_ge(s_gd1, th + 1)
                    sc.activation(p8[:, :], lgs[:, :], AF.Exp)
                    sc.drain().then_inc(s_ged, 1)

                @bG.sync
                def _(sy):
                    sy.wait_ge(s_gc, bgt + NTT)
                    sy.dma_start(cwt_d[:, toff:toff + TH],
                                 cw8[0:8, :]).then_inc(s_gw, 16)
                    sy.dma_start(cw_dbg[:, toff:toff + TH],
                                 cw8[0:8, :]).then_inc(s_gw, 16)

            for si in range(NSRC):
                has_cw = si >= 2
                e = si - 2
                pre = th * NSRC + si
                bw = 16 * (2 * MR + 1) * pre
                bmm = (4 * MR + 1) * pre
                bsi2 = 2 * MR * pre
                bt2 = 2 * MR * pre
                bcwb = 2 * (NE * th + max(0, e))
                bwd, bm2, bad = (16 * (MD + 1) * pre, (2 * MD + 1) * pre,
                                 2 * MD * pre)

                with nc.Block() as bU:
                    @bU.gpsimd
                    def _(g):
                        if has_cw:
                            g.wait_ge(s_gw, 32 * th + 16)
                            for tcs in range(2):
                                g.dma_start(
                                    cwb[:, tcs * 512:(tcs + 1) * 512],
                                    bass.AP(cwt_d, e * T + toff + tcs * 512,
                                            [[0, 128], [1, 512]])
                                ).then_inc(s_cwb, 16)
                        for mc in range(MR):
                            for wi2, wsrc in ((0, w1), (1, w3)):
                                j = mc * 2 + wi2
                                if mc >= 2:
                                    g.wait_ge(s_mm, bmm + (mc - 2) * 4 + wi2 * 2 + 2)
                                buf = j % 4
                                g.dma_start(
                                    w_sb[:, buf * KC * 128:(buf + 1) * KC * 128]
                                    .rearrange("p (kc m) -> p kc m", m=128),
                                    wsrc[si, mc].rearrange("(kc p) m -> p kc m", p=128),
                                ).then_inc(s_w, 16)
                        g.dma_start(dmy[:, :], xn[0:128, 0:128]).then_inc(s_w, 16)

                    @bU.tensor
                    def _(t):
                        for mc in range(MR):
                            for wi2 in range(2):
                                j = mc * 2 + wi2
                                t.wait_ge(s_w, bw + 16 * (j + 2))
                                buf = j % 4
                                for tcs in range(2):
                                    bank = wi2 * 2 + tcs
                                    if mc >= 1:
                                        if wi2 == 0:
                                            t.wait_ge(s_si, bsi2 + (mc - 1) * 2 + tcs + 1)
                                        else:
                                            t.wait_ge(s_t2, bt2 + (mc - 1) * 2 + tcs + 1)
                                    for kc in range(KC):
                                        inst = t.matmul(
                                            ps[:, bank * 512:(bank + 1) * 512],
                                            w_sb[:, buf * KC * 128 + kc * 128:
                                                 buf * KC * 128 + (kc + 1) * 128],
                                            xn_sb[:, kc * TH + tcs * 512:
                                                  kc * TH + (tcs + 1) * 512],
                                            start=(kc == 0), stop=(kc == KC - 1))
                                    inst.then_inc(s_mm, 1)
                        t.matmul(ps[:, 4 * 512:4 * 512 + 128], dmy[:, :], dmy[:, :],
                                 start=True, stop=True).then_inc(s_mm, 1)

                    @bU.scalar
                    def _(sc):
                        for mc in range(MR):
                            for tcs in range(2):
                                bank = tcs
                                sc.wait_ge(s_mm, bmm + mc * 4 + tcs + 2)
                                if mc >= 1:
                                    sc.wait_ge(s_t2, bt2 + (mc - 1) * 2 + tcs + 1)
                                sc.activation(t1[:, tcs * 512:(tcs + 1) * 512],
                                              ps[:, bank * 512:(bank + 1) * 512],
                                              AF.Silu)
                                sc.drain().then_inc(s_si, 1)

                    @bU.vector
                    def _(ve):
                        if has_cw:
                            ve.wait_ge(s_cwb, 16 * (bcwb + 2))
                        for mc in range(MR):
                            for tcs in range(2):
                                bank = 2 + tcs
                                ve.wait_ge(s_mm, bmm + mc * 4 + 2 + tcs + 2)
                                ve.wait_ge(s_si, bsi2 + mc * 2 + tcs + 1)
                                if not has_cw:
                                    ve.tensor_mul(
                                        geb[:, mc * TH + tcs * 512:
                                            mc * TH + (tcs + 1) * 512],
                                        t1[:, tcs * 512:(tcs + 1) * 512],
                                        ps[:, bank * 512:(bank + 1) * 512]
                                        ).then_inc(s_t2, 1)
                                else:
                                    ve.tensor_mul(t2[:, tcs * 512:(tcs + 1) * 512],
                                                  t1[:, tcs * 512:(tcs + 1) * 512],
                                                  ps[:, bank * 512:(bank + 1) * 512]
                                                  ).then_inc(s_t2, 1)
                                    ve.tensor_mul(
                                        geb[:, mc * TH + tcs * 512:
                                            mc * TH + (tcs + 1) * 512],
                                        t2[:, tcs * 512:(tcs + 1) * 512],
                                        cwb[:, tcs * 512:(tcs + 1) * 512])

                with nc.Block() as bD:
                    @bD.gpsimd
                    def _(g):
                        for md in range(MD):
                            if md >= 2:
                                g.wait_ge(s_m2, bm2 + 2 * (md - 1))
                            buf = md % 2
                            g.dma_start(
                                wd_sb[:, buf * MR * 128:(buf + 1) * MR * 128]
                                .rearrange("p (kc m) -> p kc m", m=128),
                                w2[si, md].rearrange("(kc p) m -> p kc m", p=128),
                            ).then_inc(s_wd, 16)
                        g.dma_start(dmy[:, :], xn[0:128, 0:128]).then_inc(s_wd, 16)

                    @bD.tensor
                    def _(t):
                        cnt = 0
                        for md in range(MD):
                            t.wait_ge(s_wd, bwd + 16 * (md + 2))
                            buf = md % 2
                            for tcs in range(2):
                                bank = cnt % 4
                                if cnt >= 4:
                                    t.wait_ge(s_ad, bad + cnt - 3)
                                for kh in range(MR):
                                    inst = t.matmul(
                                        ps[:, bank * 512:(bank + 1) * 512],
                                        wd_sb[:, buf * MR * 128 + kh * 128:
                                              buf * MR * 128 + (kh + 1) * 128],
                                        geb[:, kh * TH + tcs * 512:
                                            kh * TH + (tcs + 1) * 512],
                                        start=(kh == 0), stop=(kh == MR - 1))
                                inst.then_inc(s_m2, 1)
                                cnt += 1
                        t.matmul(ps[:, 4 * 512:4 * 512 + 128], dmy[:, :], dmy[:, :],
                                 start=True, stop=True).then_inc(s_m2, 1)

                    @bD.vector
                    def _(ve):
                        cnt = 0
                        for md in range(MD):
                            for tcs in range(2):
                                bank = cnt % 4
                                ve.wait_ge(s_m2, bm2 + cnt + 2)
                                asl = acc[:, md * TH + tcs * 512:
                                          md * TH + (tcs + 1) * 512]
                                if si == 0:
                                    ve.tensor_copy(asl, ps[:, bank * 512:
                                                           (bank + 1) * 512]
                                                   ).then_inc(s_ad, 1)
                                else:
                                    ve.tensor_add(asl, asl,
                                                  ps[:, bank * 512:(bank + 1) * 512]
                                                  ).then_inc(s_ad, 1)
                                cnt += 1

            with nc.Block() as bF:
                bg1f, bxsf, bof, bsof = (2 * MD * th, 64 * MD * th, 2 * MD * th,
                                         16 * 2 * MD * th)

                @bF.scalar
                def _(sc):
                    cnt = 0
                    for md in range(MD):
                        for tcs in range(2):
                            if cnt >= 2:
                                sc.wait_ge(s_o, bof + cnt - 1)
                            sc.activation(g1s[:, (cnt % 2) * 512:(cnt % 2 + 1) * 512],
                                          acc[:, md * TH + tcs * 512:
                                              md * TH + (tcs + 1) * 512], AF.Silu)
                            sc.drain().then_inc(s_g1, 1)
                            cnt += 1

                @bF.sync
                def _(sy):
                    cnt = 0
                    for md in range(MD):
                        for tcs in range(2):
                            if cnt >= 2:
                                sy.wait_ge(s_o, bof + cnt - 1)
                            b2 = cnt % 2
                            csl = slice(toff + tcs * 512, toff + tcs * 512 + 512)
                            sy.dma_start(xs_sb[:, b2 * 512:(b2 + 1) * 512],
                                         x1[md * 128:(md + 1) * 128, csl]
                                         ).then_inc(s_xs, 16)
                            sy.dma_start(ss_sb[:, b2 * 512:(b2 + 1) * 512],
                                         s1[md * 128:(md + 1) * 128, csl]
                                         ).then_inc(s_xs, 16)
                            if cnt >= 1:
                                sy.wait_ge(s_o, bof + cnt)
                                pm = (cnt - 1) % 2
                                pmd, ptcs = (cnt - 1) // 2, (cnt - 1) % 2
                                psl = slice(toff + ptcs * 512, toff + ptcs * 512 + 512)
                                sy.dma_start(out[pmd * 128:(pmd + 1) * 128, psl],
                                             o_sb[:, pm * 512:(pm + 1) * 512]
                                             ).then_inc(s_so, 16)
                            cnt += 1
                    sy.wait_ge(s_o, bof + 2 * MD)
                    pm = (2 * MD - 1) % 2
                    pmd, ptcs = (2 * MD - 1) // 2, (2 * MD - 1) % 2
                    psl = slice(toff + ptcs * 512, toff + ptcs * 512 + 512)
                    sy.dma_start(out[pmd * 128:(pmd + 1) * 128, psl],
                                 o_sb[:, pm * 512:(pm + 1) * 512]).then_inc(s_so, 16)
                    sy.wait_ge(s_so, bsof + 16 * 2 * MD)
                    if th == 1:
                        sy.wait_ge(s_gw, 64)
                        sy.wait_ge(s_t2, 2 * MR * 2 * NSRC)
                        sy.wait_ge(s_ad, 2 * MD * 2 * NSRC)
                        sy.wait_ge(s_gcp, 2 * NTT)

                @bF.vector
                def _(ve):
                    cnt = 0
                    for md in range(MD):
                        for tcs in range(2):
                            b2 = cnt % 2
                            ve.wait_ge(s_g1, bg1f + cnt + 1)
                            ve.wait_ge(s_xs, bxsf + 32 * (cnt + 1))
                            if cnt >= 2:
                                ve.wait_ge(s_so, bsof + 16 * (cnt - 1))
                            ve.tensor_mul(o_sb[:, b2 * 512:(b2 + 1) * 512],
                                          g1s[:, b2 * 512:(b2 + 1) * 512],
                                          ss_sb[:, b2 * 512:(b2 + 1) * 512])
                            ve.tensor_add(o_sb[:, b2 * 512:(b2 + 1) * 512],
                                          o_sb[:, b2 * 512:(b2 + 1) * 512],
                                          xs_sb[:, b2 * 512:(b2 + 1) * 512])
                            ve.drain().then_inc(s_o, 1)
                            cnt += 1
    return nc if ncv is None else None




def emit_transpose(nc, src_t, dst_t, A, B, dt, ident, dmy):
    NA, NB = A // 128, B // 128
    n = NA * NB
    LAG = 1
    BK = 512 if dt == F32 else 1024
    with ExitStack() as ex:
        ti = ex.enter_context(nc.sbuf_tensor("ti", [128, 4 * 128], dt))
        dld = ex.enter_context(nc.sbuf_tensor("dld", [128, 128], dt))
        to = ex.enter_context(nc.sbuf_tensor("to", [128, 4 * 128], dt))
        pst = ex.enter_context(nc.psum_tensor("pst", [128, 4 * BK], dt))
        s_l = ex.enter_context(nc.semaphore("s_l"))
        s_t = ex.enter_context(nc.semaphore("s_t"))
        s_c = ex.enter_context(nc.semaphore("s_c"))
        s_s = ex.enter_context(nc.semaphore("s_s"))
        blk = ex.enter_context(nc.Block())

        tiles = [(i, j) for i in range(NA) for j in range(NB)]

        @blk.sync
        def _(sy):
            for idx, (i, j) in enumerate(tiles):
                if idx >= 4:
                    sy.wait_ge(s_t, idx - 2)
                b = idx % 4
                sy.dma_start(ti[:, b * 128:(b + 1) * 128],
                             src_t[i * 128:(i + 1) * 128, j * 128:(j + 1) * 128]
                             ).then_inc(s_l, 16)
            sy.dma_start(dld[:, :], src_t[0:128, 0:128]).then_inc(s_l, 16)

        @blk.tensor
        def _(t):
            for idx in range(n):
                t.wait_ge(s_l, 16 * (idx + 2))
                b = idx % 4
                if idx >= 4:
                    t.wait_ge(s_c, idx - 3)
                t.transpose(pst[:, b * BK:b * BK + 128],
                            ti[:, b * 128:(b + 1) * 128], ident).then_inc(s_t, 1)
            t.wait_ge(s_c, ((n - 1) // 4) * 4 + 1)
            for _i in range(LAG):
                t.transpose(pst[:, 0:128], ti[:, 0:128], ident).then_inc(s_t, 1)

        @blk.vector
        def _(ve):
            for idx in range(n):
                ve.wait_ge(s_t, idx + 1 + LAG)
                b = idx % 4
                if idx >= 4:
                    ve.wait_ge(s_s, 16 * (idx - 3))
                ve.tensor_copy(to[:, b * 128:(b + 1) * 128],
                               pst[:, b * BK:b * BK + 128])
                ve.drain().then_inc(s_c, 1)

        @blk.gpsimd
        def _(g):
            for idx, (i, j) in enumerate(tiles):
                g.wait_ge(s_c, idx + 1)
                b = idx % 4
                g.dma_start(dst_t[j * 128:(j + 1) * 128, i * 128:(i + 1) * 128],
                            to[:, b * 128:(b + 1) * 128]).then_inc(s_s, 16)
            g.wait_ge(s_s, 16 * n)
            g.wait_ge(s_t, n + LAG)
            g.wait_ge(s_l, 16 * (n + 1))


def emit_slice(nc, offv, pairs):
    with ExitStack() as ex:
        off_sb = ex.enter_context(nc.sbuf_tensor("off_sb", [1, 1], mybir.dt.int32))
        s_o = ex.enter_context(nc.semaphore("s_o"))
        s_d = ex.enter_context(nc.semaphore("s_d"))
        blk = ex.enter_context(nc.Block())

        @blk.gpsimd
        def _(g):
            g.dma_start(off_sb[:, :], offv[:, :]).then_inc(s_o, 16)
            g.wait_ge(s_o, 16)
            with g.register("roff") as r:
                g.reg_load(r, off_sb[:1, :1])
                for src_t, dst_t in pairs:
                    g.dma_start(
                        dst_t[:, :].rearrange("(kc p) t -> p kc t", p=128),
                        bass.AP(src_t, r, [[S, 128], [128 * S, KC], [1, TPC]]),
                    ).then_inc(s_d, 16)
            g.wait_ge(s_d, 16 * len(pairs))
def build_fused(stages=99, dbg=False):
    nc = bass.Bass("TRN2", target_bir_lowering=False, detect_race_conditions=False)
    ik = {"kind": "ExternalOutput"} if dbg else {}
    x_tm = nc.dram_tensor("x_tm", [S, D], F32, kind="ExternalInput")
    x_fm = nc.dram_tensor("x_fm", [D, S], F32, kind="ExternalInput")
    offv = nc.dram_tensor("offv", [1, 1], mybir.dt.int32, kind="ExternalInput")
    wq = nc.dram_tensor("wq", [H, KC, 128, 128], F32, kind="ExternalInput")
    wk = nc.dram_tensor("wk", [G, KC, 128, 128], F32, kind="ExternalInput")
    wv = nc.dram_tensor("wv", [G, KC, 128, 128], F32, kind="ExternalInput")
    bqT = nc.dram_tensor("bqT", [128, H], F32, kind="ExternalInput")
    bkT = nc.dram_tensor("bkT", [128, G], F32, kind="ExternalInput")
    bv = nc.dram_tensor("bv", [1, G * 128], F32, kind="ExternalInput")
    wg0 = nc.dram_tensor("wg0", [FF0 // 128, D, 128], F32, kind="ExternalInput")
    wu0 = nc.dram_tensor("wu0", [FF0 // 128, D, 128], F32, kind="ExternalInput")
    wd0 = nc.dram_tensor("wd0", [D // 128, FF0, 128], F32, kind="ExternalInput")
    wi = nc.dram_tensor("wi", [16, D, 128], F32, kind="ExternalInput")
    wf = nc.dram_tensor("wf", [16, D, 128], F32, kind="ExternalInput")
    wgp = nc.dram_tensor("wgp", [16, D, 128], F32, kind="ExternalInput")
    wn = nc.dram_tensor("wn", [1, D], F32, kind="ExternalInput")
    wo_w = nc.dram_tensor("wo_w", [KC, D, 128], F32, kind="ExternalInput")
    gw = nc.dram_tensor("gw", [KC, 128, 8], F32, kind="ExternalInput")
    mw1 = nc.dram_tensor("mw1", [10, FR // 128, D, 128], F32, kind="ExternalInput")
    mw3 = nc.dram_tensor("mw3", [10, FR // 128, D, 128], F32, kind="ExternalInput")
    mw2 = nc.dram_tensor("mw2", [10, D // 128, FR, 128], F32, kind="ExternalInput")
    out = nc.dram_tensor("out", [D, TPC], F32, kind="ExternalOutput")
    cw_dbg = nc.dram_tensor("cw_dbg", [NE, TPC], F32, kind="ExternalOutput")
    xn_tm = nc.dram_tensor("xn_tm", [S, D], F32, **ik)
    xn_fm = nc.dram_tensor("xn_fm", [D, S], F32, **ik)
    q_d = nc.dram_tensor("q_d", [H, 128, S], BF16, **ik)
    kt_d = nc.dram_tensor("kt_d", [G, 128, S], BF16, **ik)
    v_d = nc.dram_tensor("v_d", [G, S, 128], BF16, **ik)
    s_d = nc.dram_tensor("s_d", [D, S], F32, **ik)
    x1_fm = nc.dram_tensor("x1_fm", [D, S], F32, **ik)
    x1_tm = nc.dram_tensor("x1_tm", [S, D], F32, **ik)
    xn1_tm = nc.dram_tensor("xn1_tm", [S, D], F32, **ik)
    xn1_fm = nc.dram_tensor("xn1_fm", [D, S], F32, **ik)
    xg_fm = nc.dram_tensor("xg_fm", [D, S], F32, **ik)
    x1_own = nc.dram_tensor("x1_own", [D, TPC], F32, **ik)
    xn1_own = nc.dram_tensor("xn1_own", [D, TPC], F32, **ik)
    xg_own_fm = nc.dram_tensor("xg_own_fm", [D, TPC], F32, **ik)
    xg_own_tm = nc.dram_tensor("xg_own_tm", [TPC, D], F32, **ik)
    y_tm = nc.dram_tensor("y_tm", [TPC, D], BF16, **ik)
    y_fm = nc.dram_tensor("y_fm", [D, TPC], BF16, **ik)
    s1_own = nc.dram_tensor("s1_own", [D, TPC], F32, **ik)
    cwt_d = nc.dram_tensor("cwt_d", [NE, TPC], F32, **ik)

    exg = ExitStack()
    ncg = NCView(nc, "g_")
    dmy, _pw = warmup(ncg, exg, None)
    cst = setup_consts(ncg, exg, [1.0 / D, EPS, -ISCALE, -1.0])
    idf32 = exg.enter_context(nc.sbuf_tensor("g_idf32", [128, 128], F32))
    idbf = exg.enter_context(nc.sbuf_tensor("g_idbf", [128, 128], BF16))
    with nc.Block() as bi:
        @bi.gpsimd
        def _(g):
            g.memset(idf32[:, :], 0.0)
            g.affine_select(out=idf32[:, :], in_=idf32[:, :],
                            compare_op=ALU.not_equal, fill=1.0, base=0,
                            pattern=[[-1, 128]], channel_multiplier=1)
            g.memset(idbf[:, :], 0.0)
            g.affine_select(out=idbf[:, :], in_=idbf[:, :],
                            compare_op=ALU.not_equal, fill=1.0, base=0,
                            pattern=[[-1, 128]], channel_multiplier=1)
    shared = {"dmy": dmy, "cst": cst, "ps_wrm": None}

    _nst = [0]

    def _stage(f):
        _nst[0] += 1
        if _nst[0] > stages:
            return
        with nc.cleanup_on_exit():
            f()

    _stage(lambda: build_ln(S, ncv=NCView(nc, "l0_"),
                            io={"x": x_tm, "xn": xn_tm, **shared}))
    _stage(lambda: emit_transpose(NCView(nc, "t1_"), xn_tm, xn_fm, S, D, F32,
                                  idf32[:, :], dmy))
    _stage(lambda: build_qkv(ncv=NCView(nc, "qk_"), T=S, io={
        "xn": xn_fm, "wq": wq, "wk": wk, "wv": wv, "bqT": bqT, "bkT": bkT,
        "bv": bv, "q": q_d, "k": kt_d, "v": v_d, **shared}))
    _stage(lambda: build_attn(nheads=H, ngrp=G, ncv=NCView(nc, "at_"), io={
        "q": q_d, "kt": kt_d, "v": v_d, "s_out": s_d, **shared}))
    _stage(lambda: build_ffn0(ncv=NCView(nc, "f0_"), T=S, io={
        "xn": xn_fm, "xr": x_fm, "sr": s_d, "wg": wg0, "wu": wu0, "wd": wd0,
        "x1": x1_fm, **shared}))
    _stage(lambda: emit_transpose(NCView(nc, "t2_"), x1_fm, x1_tm, D, S, F32,
                                  idf32[:, :], dmy))
    _stage(lambda: build_ln(S, ncv=NCView(nc, "l1_"),
                            io={"x": x1_tm, "xn": xn1_tm, **shared}))
    _stage(lambda: emit_transpose(NCView(nc, "t3_"), xn1_tm, xn1_fm, S, D, F32,
                                  idf32[:, :], dmy))
    _stage(lambda: build_scan(ncv=NCView(nc, "sc_"), nem=16, io={
        "xn": xn1_fm, "wi": wi, "wf": wf, "wg": wgp, "xg": xg_fm, **shared}))
    _stage(lambda: emit_slice(NCView(nc, "sl_"), offv,
               [(x1_fm, x1_own), (xn1_fm, xn1_own), (xg_fm, xg_own_fm)]))
    _stage(lambda: emit_transpose(NCView(nc, "t4_"), xg_own_fm, xg_own_tm, D,
                                  TPC, F32, idf32[:, :], dmy))
    _stage(lambda: build_rms(ncv=NCView(nc, "rm_"), io={
        "xg": xg_own_tm, "wn": wn, "y": y_tm, **shared}))
    _stage(lambda: emit_transpose(NCView(nc, "t5_"), y_tm, y_fm, TPC, D, BF16,
                                  idbf[:, :], dmy))
    _stage(lambda: build_wo(ncv=NCView(nc, "wo_"), io={
        "yf": y_fm, "wo": wo_w, "s1": s1_own, **shared}))
    _stage(lambda: build_moe(ncv=NCView(nc, "mo_"), io={
        "xn": xn1_own, "x1": x1_own, "s1": s1_own, "gw": gw, "w1": mw1,
        "w3": mw3, "w2": mw2, "out": out, "cw_dbg": cw_dbg, "cwt_d": cwt_d,
        **shared}))
    exg.close()
    return nc


def kernel_fused(L0_attn_x, L0_attn_Wq, L0_attn_bq, L0_attn_Wk, L0_attn_bk,
                 L0_attn_Wv, L0_attn_bv, L0_ffn_w_gate, L0_ffn_w_up,
                 L0_ffn_w_down, L1_attn_Wi, L1_attn_Wf, L1_attn_Wg,
                 L1_attn_w_norm, L1_attn_Wo, L1_ffn_gate_w, L1_ffn_sw1,
                 L1_ffn_sw2, L1_ffn_sw3, L1_ffn_rw1, L1_ffn_rw2, L1_ffn_rw3):
    f32 = np.float32
    x_full = np.asarray(L0_attn_x, dtype=f32)
    nc = _get("fused", build_fused)
    w1_t, w3_t, w2_t = moe_weight_prep(L1_ffn_sw1, L1_ffn_sw3, L1_ffn_sw2,
                                       L1_ffn_rw1, L1_ffn_rw3, L1_ffn_rw2)
    wshared = {
        "wq": cc(np.asarray(L0_attn_Wq, f32).reshape(H, KC, 128, 128)),
        "wk": cc(np.asarray(L0_attn_Wk, f32).reshape(G, KC, 128, 128)),
        "wv": cc(np.asarray(L0_attn_Wv, f32).reshape(G, KC, 128, 128)),
        "bqT": cc(np.asarray(L0_attn_bq, f32).T),
        "bkT": cc(np.asarray(L0_attn_bk, f32).T),
        "bv": cc(np.asarray(L0_attn_bv, f32).reshape(1, G * 128)),
        "wg0": tile_w(np.asarray(L0_ffn_w_gate, f32).T),
        "wu0": tile_w(np.asarray(L0_ffn_w_up, f32).T),
        "wd0": tile_w(np.asarray(L0_ffn_w_down, f32).T),
        "wi": tile_w(np.asarray(L1_attn_Wi, f32).T),
        "wf": tile_w(np.asarray(L1_attn_Wf, f32).T),
        "wgp": tile_w(np.asarray(L1_attn_Wg, f32).T),
        "wn": cc(np.asarray(L1_attn_w_norm, f32).reshape(1, D)),
        "wo_w": tile_w(np.asarray(L1_attn_Wo, f32).T),
        "gw": cc(np.asarray(L1_ffn_gate_w, f32).reshape(KC, 128, NE)),
        "mw1": w1_t, "mw3": w3_t, "mw2": w2_t,
    }
    in_maps = []
    for c in range(NCORE):
        b, hh = c // 2, c % 2
        in_maps.append({
            "x_tm": cc(x_full[b]),
            "x_fm": cc(x_full[b].T),
            "offv": np.array([[hh * TPC]], dtype=np.int32),
            **wshared})
    outs = _run(nc, in_maps)
    out = np.empty((B, S, D), dtype=f32)
    for c in range(NCORE):
        b, hh = c // 2, c % 2
        out[b, hh * TPC:(hh + 1) * TPC] = outs[c]["out"].T
    return out


_cache = {}


def _get(name, builder):
    if name not in _cache:
        _cache[name] = builder()
    return _cache[name]


def _run(nc, in_maps):
    res = run_bass_kernel_spmd(nc, in_maps, core_ids=list(range(NCORE)))
    return res.results


def moe_weight_prep(sw1, sw3, sw2, rw1, rw3, rw2):
    f32 = np.float32
    sw1 = np.asarray(sw1, f32); sw3 = np.asarray(sw3, f32); sw2 = np.asarray(sw2, f32)
    rw1 = np.asarray(rw1, f32); rw3 = np.asarray(rw3, f32); rw2 = np.asarray(rw2, f32)
    w1 = [tile_w(sw1[:, h * FR:(h + 1) * FR]) for h in range(2)] + \
         [tile_w(rw1[e]) for e in range(NE)]
    w3 = [tile_w(sw3[:, h * FR:(h + 1) * FR]) for h in range(2)] + \
         [tile_w(rw3[e]) for e in range(NE)]
    w2 = [tile_w(sw2[h * FR:(h + 1) * FR, :]) for h in range(2)] + \
         [tile_w(rw2[e]) for e in range(NE)]
    return cc(np.stack(w1)), cc(np.stack(w3)), cc(np.stack(w2))


def kernel_v0(L0_attn_x, L0_attn_Wq, L0_attn_bq, L0_attn_Wk, L0_attn_bk,
           L0_attn_Wv, L0_attn_bv, L0_ffn_w_gate, L0_ffn_w_up, L0_ffn_w_down,
           L1_attn_Wi, L1_attn_Wf, L1_attn_Wg, L1_attn_w_norm, L1_attn_Wo,
           L1_ffn_gate_w, L1_ffn_sw1, L1_ffn_sw2, L1_ffn_sw3,
           L1_ffn_rw1, L1_ffn_rw2, L1_ffn_rw3):
    f32 = np.float32
    x_full = np.asarray(L0_attn_x, dtype=f32)

    def tok_slices(arr_bsd):
        return [cc(arr_bsd[c // 2, (c % 2) * TPC:(c % 2 + 1) * TPC]) for c in range(NCORE)]

    nc_ln = _get("ln", lambda: build_ln(TPC))
    outs = _run(nc_ln, [{"x": s} for s in tok_slices(x_full)])
    xn_tm = [o["xn"] for o in outs]
    xn_fm = [cc(a.T) for a in xn_tm]
    np.save("/tmp/st_xn.npy", np.stack(xn_tm))

    wq_t = cc(np.asarray(L0_attn_Wq, f32).reshape(H, KC, 128, 128))
    wk_t = cc(np.asarray(L0_attn_Wk, f32).reshape(G, KC, 128, 128))
    wv_t = cc(np.asarray(L0_attn_Wv, f32).reshape(G, KC, 128, 128))
    bqT = cc(np.asarray(L0_attn_bq, f32).T)
    bkT = cc(np.asarray(L0_attn_bk, f32).T)
    bvr = cc(np.asarray(L0_attn_bv, f32).reshape(1, G * 128))
    nc_qkv = _get("qkv", build_qkv)
    outs = _run(nc_qkv, [{"xn": xn_fm[c], "wq": wq_t, "wk": wk_t, "wv": wv_t,
                          "bqT": bqT, "bkT": bkT, "bv": bvr} for c in range(NCORE)])
    q_b = [np.concatenate([outs[2 * b]["q"], outs[2 * b + 1]["q"]], axis=2)
           for b in range(B)]
    k_b = [np.concatenate([outs[2 * b]["k"], outs[2 * b + 1]["k"]], axis=2)
           for b in range(B)]
    v_b = [np.concatenate([outs[2 * b]["v"], outs[2 * b + 1]["v"]], axis=1)
           for b in range(B)]
    np.save("/tmp/st_q.npy", np.stack(q_b).astype(np.float32))
    np.save("/tmp/st_k.npy", np.stack(k_b).astype(np.float32))
    np.save("/tmp/st_v.npy", np.stack(v_b).astype(np.float32))

    nc_at = _get("attn", build_attn)
    in_maps = []
    for c in range(NCORE):
        b, hh = c // 2, c % 2
        in_maps.append({"q": cc(q_b[b][8 * hh:8 * hh + 8]),
                        "kt": cc(k_b[b][2 * hh:2 * hh + 2]),
                        "v": cc(v_b[b][2 * hh:2 * hh + 2])})
    outs = _run(nc_at, in_maps)
    s_fm_b = [np.concatenate([outs[2 * b]["s_out"], outs[2 * b + 1]["s_out"]],
                             axis=0).reshape(D, S) for b in range(B)]
    np.save("/tmp/st_s.npy", np.stack(s_fm_b))

    wg_t = tile_w(np.asarray(L0_ffn_w_gate, f32).T)
    wu_t = tile_w(np.asarray(L0_ffn_w_up, f32).T)
    wd_t = tile_w(np.asarray(L0_ffn_w_down, f32).T)
    nc_f0 = _get("ffn0", build_ffn0)
    in_maps = []
    for c in range(NCORE):
        b, hh = c // 2, c % 2
        tsl = slice(hh * TPC, (hh + 1) * TPC)
        in_maps.append({"xn": xn_fm[c],
                        "xr": cc(x_full[b].T[:, tsl]),
                        "sr": cc(s_fm_b[b][:, tsl]),
                        "wg": wg_t, "wu": wu_t, "wd": wd_t})
    outs = _run(nc_f0, in_maps)
    x1_fm = [outs[c]["x1"] for c in range(NCORE)]
    np.save("/tmp/st_x1.npy", np.stack(x1_fm))

    x1_tm = [cc(a.T) for a in x1_fm]
    outs = _run(nc_ln, [{"x": s} for s in x1_tm])
    xn1_tm = [o["xn"] for o in outs]
    np.save("/tmp/st_xn1.npy", np.stack(xn1_tm))
    xn1_fm = [cc(a.T) for a in xn1_tm]
    xn1_fm_b = [cc(np.concatenate([xn1_fm[2 * b], xn1_fm[2 * b + 1]], axis=1))
                for b in range(B)]

    wi_t = tile_w(np.asarray(L1_attn_Wi, f32).T)
    wf_t = tile_w(np.asarray(L1_attn_Wf, f32).T)
    wg1_t = tile_w(np.asarray(L1_attn_Wg, f32).T)
    nc_sc = _get("scan", build_scan)
    in_maps = []
    for c in range(NCORE):
        b, eh = c // 2, c % 2
        sl = slice(eh * 8, eh * 8 + 8)
        in_maps.append({"xn": xn1_fm_b[b], "wi": cc(wi_t[sl]),
                        "wf": cc(wf_t[sl]), "wg": cc(wg1_t[sl])})
    outs = _run(nc_sc, in_maps)
    xg_fm_b = [np.concatenate([outs[2 * b]["xg"], outs[2 * b + 1]["xg"]], axis=0)
               for b in range(B)]
    np.save("/tmp/st_xg.npy", np.stack(xg_fm_b))

    nc_rm = _get("rms", build_rms)
    wn = cc(np.asarray(L1_attn_w_norm, f32).reshape(1, D))
    in_maps = []
    for c in range(NCORE):
        b, hh = c // 2, c % 2
        in_maps.append({"xg": cc(xg_fm_b[b][:, hh * TPC:(hh + 1) * TPC].T),
                        "wn": wn})
    outs = _run(nc_rm, in_maps)
    y_tm = [outs[c]["y"] for c in range(NCORE)]
    np.save("/tmp/st_y.npy", np.stack([a.astype(np.float32) for a in y_tm]))

    wo_t = tile_w(np.asarray(L1_attn_Wo, f32).T)
    nc_wo = _get("wo", build_wo)
    outs = _run(nc_wo, [{"yf": cc(y_tm[c].T), "wo": wo_t} for c in range(NCORE)])
    s1_fm = [outs[c]["s1"] for c in range(NCORE)]
    np.save("/tmp/st_s1.npy", np.stack(s1_fm))

    gw_t = cc(np.asarray(L1_ffn_gate_w, f32).reshape(KC, 128, NE))
    w1_t, w3_t, w2_t = moe_weight_prep(L1_ffn_sw1, L1_ffn_sw3, L1_ffn_sw2,
                                       L1_ffn_rw1, L1_ffn_rw3, L1_ffn_rw2)
    nc_mo = _get("moe", build_moe)
    in_maps = []
    for c in range(NCORE):
        in_maps.append({"xn": xn1_fm[c], "x1": x1_fm[c], "s1": s1_fm[c],
                        "gw": gw_t, "w1": w1_t, "w3": w3_t, "w2": w2_t})
    outs = _run(nc_mo, in_maps)

    np.save("/tmp/st_cw.npy", np.stack([outs[c]["cw_dbg"] for c in range(NCORE)]))
    out = np.empty((B, S, D), dtype=f32)
    for c in range(NCORE):
        b, hh = c // 2, c % 2
        out[b, hh * TPC:(hh + 1) * TPC] = outs[c]["out"].T
    np.save("/tmp/st_out.npy", out)
    return out


kernel = kernel_fused
